# version 04
# baseline (speedup 1.0000x reference)
import contextlib
import math
import numpy as np
import ml_dtypes
import concourse.bass as bass
import concourse.mybir as mybir
from concourse.bass_utils import run_bass_kernel_spmd

F32 = mybir.dt.float32
BF16 = mybir.dt.bfloat16
AF = mybir.ActivationFunctionType
ALU = mybir.AluOpType
AX = mybir.AxisListType

B, T, C = 16, 2048, 2048
H, DH = 32, 64
I = 5632
NCORE = 8
SEQ_PER_CORE = B // NCORE
OC = C // 128
CC = C // 128
ICN = I // 128
NT = T // 128
EPS = 1e-5

DEV_PHASES = None
DEV_C_HEADS = None
DEV_DBG = False
DEV_DBG2 = False


class Sem:
    def __init__(self, nc, name):
        self._cm = nc.semaphore(name)
        self.h = self._cm.__enter__()
        self.v = 0

    def inc(self, n=1):
        self.v += n
        return self.v


def _emit(x):
    return x


def build_nc():
    nc = bass.Bass("TRN2", target_bir_lowering=False)

    x_in = nc.dram_tensor("x_in", [SEQ_PER_CORE, T, C], F32, kind="ExternalInput").ap()
    w_t = {}
    for nm in ("wq", "wk", "wv", "wo"):
        w_t[nm] = nc.dram_tensor(nm + "_t", [OC, 128, C], F32, kind="ExternalInput").ap()
    w_t["gw"] = nc.dram_tensor("gw_t", [ICN, 128, C], F32, kind="ExternalInput").ap()
    w_t["uw"] = nc.dram_tensor("uw_t", [ICN, 128, C], F32, kind="ExternalInput").ap()
    w_t["dw"] = nc.dram_tensor("dw_t", [OC, 128, I], F32, kind="ExternalInput").ap()
    b_c = {}
    for nm in ("bq", "bk", "bv", "bo"):
        b_c[nm] = nc.dram_tensor(nm + "_c", [128, OC], F32, kind="ExternalInput").ap()
    gs_c = nc.dram_tensor("gs_c", [128, ICN], F32, kind="ExternalInput").ap()
    us_c = nc.dram_tensor("us_c", [128, ICN], F32, kind="ExternalInput").ap()
    ds_c = nc.dram_tensor("ds_c", [128, OC], F32, kind="ExternalInput").ap()
    cosf = nc.dram_tensor("cosf", [128, T], F32, kind="ExternalInput").ap()
    sinf = nc.dram_tensor("sinf", [128, T], F32, kind="ExternalInput").ap()
    maskt = nc.dram_tensor("maskt", [128, 128], F32, kind="ExternalInput").ap()
    rpermt = nc.dram_tensor("rpermt", [128, 128], BF16, kind="ExternalInput").ap()
    idbt = nc.dram_tensor("idbt", [128, 128], BF16, kind="ExternalInput").ap()
    idft = nc.dram_tensor("idft", [128, 128], F32, kind="ExternalInput").ap()
    epst = nc.dram_tensor("epst", [128, 1], F32, kind="ExternalInput").ap()

    y_out = nc.dram_tensor("y_out", [SEQ_PER_CORE, T, C], F32, kind="ExternalOutput").ap()

    w_b = {}
    for nm in ("wq", "wk", "wv", "wo"):
        w_b[nm] = nc.dram_tensor(nm + "_b", [OC, 128, C], BF16, kind="Internal").ap()
    w_b["gw"] = nc.dram_tensor("gw_b", [ICN, 128, C], BF16, kind="Internal").ap()
    w_b["uw"] = nc.dram_tensor("uw_b", [ICN, 128, C], BF16, kind="Internal").ap()
    w_b["dw"] = nc.dram_tensor("dw_b", [OC, 128, I], BF16, kind="Internal").ap()
    _scrkind = "ExternalOutput" if DEV_DBG else "Internal"
    k_scr = nc.dram_tensor("k_scr", [OC, 128, T], BF16, kind=_scrkind).ap()
    q_scr = nc.dram_tensor("q_scr", [OC, 128, T], BF16, kind=_scrkind).ap()
    attn_scr = nc.dram_tensor("attn_scr", [OC, 128, T], BF16, kind=_scrkind).ap()
    y_scr = nc.dram_tensor("y_scr", [NT, 128, C], F32, kind=_scrkind).ap()

    es = contextlib.ExitStack()

    xin = es.enter_context(nc.sbuf_tensor("xin", [128, 2 * C], F32))
    sdump = es.enter_context(nc.sbuf_tensor("sdump", [128, C], BF16))
    st_sum = es.enter_context(nc.sbuf_tensor("st_sum", [128, 8], F32))
    st_sq = es.enter_context(nc.sbuf_tensor("st_sq", [128, 8], F32))
    st_mean = es.enter_context(nc.sbuf_tensor("st_mean", [128, 8], F32))
    st_em2 = es.enter_context(nc.sbuf_tensor("st_em2", [128, 8], F32))
    st_negv = es.enter_context(nc.sbuf_tensor("st_negv", [128, 8], F32))
    st_lnv = es.enter_context(nc.sbuf_tensor("st_lnv", [128, 8], F32))
    st_rstd = es.enter_context(nc.sbuf_tensor("st_rstd", [128, 8], F32))
    st_mr = es.enter_context(nc.sbuf_tensor("st_mr", [128, 8], F32))
    st_negmr = es.enter_context(nc.sbuf_tensor("st_negmr", [128, 8], F32))
    sb_bias = {nm: es.enter_context(nc.sbuf_tensor("sb_" + nm, [128, OC], F32))
               for nm in ("bq", "bk", "bv", "bo")}
    sb_gs = es.enter_context(nc.sbuf_tensor("sb_gs", [128, ICN], F32))
    sb_us = es.enter_context(nc.sbuf_tensor("sb_us", [128, ICN], F32))
    sb_ds = es.enter_context(nc.sbuf_tensor("sb_ds", [128, OC], F32))
    smask = es.enter_context(nc.sbuf_tensor("smask", [128, 128], F32))
    srp = es.enter_context(nc.sbuf_tensor("srp", [128, 128], BF16))
    sidb = es.enter_context(nc.sbuf_tensor("sidb", [128, 128], BF16))
    sidf = es.enter_context(nc.sbuf_tensor("sidf", [128, 128], F32))
    s_eps = es.enter_context(nc.sbuf_tensor("s_eps", [128, 1], F32))
    fV = es.enter_context(nc.sbuf_tensor("fV", [128, 1], F32))
    fV2 = es.enter_context(nc.sbuf_tensor("fV2", [128, 1], F32))
    fS = es.enter_context(nc.sbuf_tensor("fS", [128, 1], F32))
    fS2 = es.enter_context(nc.sbuf_tensor("fS2", [128, 1], F32))
    sdbg = es.enter_context(nc.sbuf_tensor("sdbg", [128, 64], F32))
    c_mkb = es.enter_context(nc.sbuf_tensor("c_mkb", [128, 24], F32))
    c_rkb = es.enter_context(nc.sbuf_tensor("c_rkb", [128, 24], F32))
    c_scl = es.enter_context(nc.sbuf_tensor("c_scl", [128, 24], F32))
    c_negmq = es.enter_context(nc.sbuf_tensor("c_negmq", [128, 12], F32))
    c_d = es.enter_context(nc.sbuf_tensor("c_d", [128, 24], F32))
    c_e = es.enter_context(nc.sbuf_tensor("c_e", [128, 24], F32))
    c_rE = es.enter_context(nc.sbuf_tensor("c_rE", [128, 24], F32))
    c_rq = es.enter_context(nc.sbuf_tensor("c_rq", [128, 12], F32))

    PH = Sem(nc, "PH")
    LS = [Sem(nc, "LS0"), Sem(nc, "LS1")]
    LX = [Sem(nc, "LX0"), Sem(nc, "LX1")]
    LO = Sem(nc, "LO")
    SS = [Sem(nc, "SS0"), Sem(nc, "SS1")]
    SO = Sem(nc, "SO")
    TA = Sem(nc, "TA")
    TB = Sem(nc, "TB")
    TC = Sem(nc, "TC")
    VA = Sem(nc, "VA")
    VB = Sem(nc, "VB")
    VC = Sem(nc, "VC")
    VD = Sem(nc, "VD")
    SA = Sem(nc, "SA")
    SB = Sem(nc, "SB")
    SD = Sem(nc, "SD")
    SE = Sem(nc, "SE")
    SF = Sem(nc, "SF")
    VF = Sem(nc, "VF")

    def dma_ld(eng, sem, out, in_):
        eng.dma_start(out=out, in_=in_).then_inc(sem.h, 16)
        return sem.inc(16)

    def sc_fence(scalar):
        scalar.activation(out=fS[:], in_=fS2[:], func=AF.Identity).then_inc(PH.h, 1)

    def ve_fence(vector, te_final):
        for sem, val in te_final:
            vector.wait_ge(sem.h, val)
        vector.tensor_copy(out=fV[:], in_=fV2[:]).then_inc(PH.h, 1)
        vector.tensor_copy(out=fV2[:], in_=fV[:]).then_inc(PH.h, 1)

    def store_wait_all(scalar):
        scalar.wait_ge(SS[0].h, SS[0].v)
        scalar.wait_ge(SS[1].h, SS[1].v)
        scalar.wait_ge(SO.h, SO.v)

    block = es.enter_context(nc.Block())

    phase_idx = [0]

    def want(name):
        return DEV_PHASES is None or name in DEV_PHASES

    def barrier_start(eng):
        eng.wait_ge(PH.h, 4 * phase_idx[0])

    def phase_done(last_insts):
        phase_idx[0] += 1
        PH.v += 4

    def ve_stats_a(vector, src_ap, par):
        op = vector.tensor_reduce(out=st_sum[:, 4 * par:4 * par + 1], in_=src_ap,
                                  axis=AX.X, op=ALU.add)
        op.then_inc(VF.h, 1)
        VF.inc()
        return op

    def ve_stats_b(vector, par):
        vector.wait_ge(VF.h, VF.v)
        vector.tensor_scalar(out=st_mean[:, 4 * par:4 * par + 1], in0=st_sum[:, 4 * par:4 * par + 1],
                             scalar1=1.0 / C, scalar2=None, op0=ALU.mult)
        op = vector.tensor_scalar(out=st_em2[:, 4 * par:4 * par + 1], in0=st_sq[:, 4 * par:4 * par + 1],
                                  scalar1=1.0 / C, scalar2=None, op0=ALU.mult)
        op.then_inc(VF.h, 1)
        VF.inc()
        vector.wait_ge(VF.h, VF.v)
        return vector.scalar_tensor_tensor(
            out=st_negv[:, 4 * par:4 * par + 1], in0=st_mean[:, 4 * par:4 * par + 1],
            scalar=st_mean[:, 4 * par:4 * par + 1], in1=st_em2[:, 4 * par:4 * par + 1],
            op0=ALU.mult, op1=ALU.subtract)

    def sc_stats_sq(scalar, src_ap, par):
        scalar.activation(out=sdump[:], in_=src_ap, func=AF.Square,
                          accum_out=st_sq[:, 4 * par:4 * par + 1])
        return scalar.activation(out=fS[:], in_=st_sq[:, 4 * par:4 * par + 1],
                                 func=AF.Identity)

    def sc_stats_norm(scalar, src_ap, out_ap, par):
        scalar.activation(out=st_lnv[:, 4 * par:4 * par + 1], in_=st_negv[:, 4 * par:4 * par + 1],
                          func=AF.Ln, scale=-1.0, bias=s_eps[:, 0:1])
        op = scalar.activation(out=st_rstd[:, 4 * par:4 * par + 1], in_=st_lnv[:, 4 * par:4 * par + 1],
                               func=AF.Exp, scale=-0.5)
        op.then_inc(SF.h, 1)
        SF.inc()
        scalar.wait_ge(SF.h, SF.v)
        scalar.activation(out=st_mr[:, 4 * par:4 * par + 1], in_=st_mean[:, 4 * par:4 * par + 1],
                          func=AF.Identity, scale=st_rstd[:, 4 * par:4 * par + 1])
        op = scalar.activation(out=st_negmr[:, 4 * par:4 * par + 1], in_=st_mr[:, 4 * par:4 * par + 1],
                               func=AF.Identity, scale=-1.0)
        op.then_inc(SF.h, 1)
        SF.inc()
        scalar.wait_ge(SF.h, SF.v)
        return scalar.activation(out=out_ap, in_=src_ap, func=AF.Identity,
                                 scale=st_rstd[:, 4 * par:4 * par + 1], bias=st_negmr[:, 4 * par:4 * par + 1])

    prep_list = []
    prep_list.append((w_t["wq"], w_b["wq"], OC, C, 0.125))
    for nm in ("wk", "wv", "wo"):
        prep_list.append((w_t[nm], w_b[nm], OC, C, 1.0))
    prep_list.append((w_t["gw"], w_b["gw"], ICN, C, 1.0))
    prep_list.append((w_t["uw"], w_b["uw"], ICN, C, 1.0))
    prep_list.append((w_t["dw"], w_b["dw"], OC, I, 1.0))
    prep_chunks = []
    for src, dst, nch, width, scale in prep_list:
        for ch in range(nch):
            prep_chunks.append((src, dst, ch, width, scale))
    NPREP = len(prep_chunks)

    if want("prep"):
        pes = contextlib.ExitStack()
        pf = pes.enter_context(nc.sbuf_tensor("pf", [128, 2 * I], F32))
        pb = pes.enter_context(nc.sbuf_tensor("pb", [128, 2 * I], BF16))
        va0, lo0 = VA.v, LO.v
        ls0 = [LS[0].v, LS[1].v]
        ss0 = [SS[0].v, SS[1].v]

        @block.sync
        def _(sync):
            barrier_start(sync)
            dma_ld(sync, LO, smask[:], maskt)
            dma_ld(sync, LO, srp[:], rpermt)
            dma_ld(sync, LO, sidb[:], idbt)
            dma_ld(sync, LO, sidf[:], idft)
            for nm in ("bq", "bk", "bv", "bo"):
                dma_ld(sync, LO, sb_bias[nm][:], b_c[nm])
            dma_ld(sync, LO, sb_gs[:], gs_c)
            dma_ld(sync, LO, sb_us[:], us_c)
            dma_ld(sync, LO, sb_ds[:], ds_c)
            dma_ld(sync, LO, s_eps[:], epst)
            dma_ld(sync, LO, fV2[:], epst)
            dma_ld(sync, LO, fS2[:], epst)
            for n, (src, dst, ch, width, scale) in enumerate(prep_chunks):
                if n >= 2:
                    sync.wait_ge(VA.h, va0 + n - 1)
                dma_ld(sync, LS[n % 2], pf[:, (n % 2) * I:(n % 2) * I + width], src[ch])
            sync.nop().then_inc(PH.h, 1)

        @block.vector
        def _(vector):
            barrier_start(vector)
            vector.wait_ge(LO.h, lo0 + 16 * 14)
            for n, (src, dst, ch, width, scale) in enumerate(prep_chunks):
                vector.wait_ge(LS[n % 2].h, ls0[n % 2] + 16 * (n // 2 + 1))
                if n >= 2:
                    vector.wait_ge(SS[n % 2].h, ss0[n % 2] + 16 * ((n - 2) // 2 + 1))
                op = vector.tensor_scalar(out=pb[:, (n % 2) * I:(n % 2) * I + width],
                                          in0=pf[:, (n % 2) * I:(n % 2) * I + width],
                                          scalar1=scale, scalar2=None, op0=ALU.mult)
                op.then_inc(VA.h, 1)
                VA.inc()
            op = vector.tensor_scalar(out=sb_bias["bq"][:], in0=sb_bias["bq"][:],
                                      scalar1=0.125, scalar2=None, op0=ALU.mult)
            ve_fence(vector, [])

        @block.scalar
        def _(scalar):
            barrier_start(scalar)
            for n, (src, dst, ch, width, scale) in enumerate(prep_chunks):
                scalar.wait_ge(VA.h, va0 + n + 1)
                scalar.dma_start(out=dst[ch], in_=pb[:, (n % 2) * I:(n % 2) * I + width]
                                 ).then_inc(SS[n % 2].h, 16)
                SS[n % 2].inc(16)
            store_wait_all(scalar)
            sc_fence(scalar)

        @block.tensor
        def _(tensor):
            barrier_start(tensor)

        pes.close()
        phase_done(None)

    for s in range(SEQ_PER_CORE):
        seq_es = contextlib.ExitStack()
        ln1es = contextlib.ExitStack()
        ln1_fm = ln1es.enter_context(nc.sbuf_tensor(f"ln1_fm_{s}", [128, OC * T], BF16))

        if want("A"):
            aes = contextlib.ExitStack()
            ln1_tm = aes.enter_context(nc.sbuf_tensor(f"ln1tm_{s}", [128, 2 * C], BF16))
            ppA0 = aes.enter_context(nc.psum_tensor(f"ppA0_{s}", [128, 512], BF16))
            ppA1 = aes.enter_context(nc.psum_tensor(f"ppA1_{s}", [128, 512], BF16))
            ppAs = [ppA0, ppA1]
            va0, vb0, sa0, ta0 = VA.v, VB.v, SA.v, TA.v
            ls0 = [LS[0].v, LS[1].v]

            @block.sync
            def _(sync):
                barrier_start(sync)
                for t in range(NT):
                    if t >= 2:
                        sync.wait_ge(SA.h, sa0 + 2 * t - 2)
                    dma_ld(sync, LS[t % 2], xin[:, (t % 2) * C:(t % 2 + 1) * C],
                           x_in[s, t * 128:(t + 1) * 128, :])
                sync.nop().then_inc(PH.h, 1)

            @block.vector
            def _(vector):
                barrier_start(vector)
                for t in range(NT):
                    par = t % 2
                    xt = xin[:, par * C:(par + 1) * C]
                    vector.wait_ge(LS[par].h, ls0[par] + 16 * (t // 2 + 1))
                    if t >= 2:
                        vector.wait_ge(SA.h, sa0 + 2 * t - 2)
                    ve_stats_a(vector, xt, par)
                    vector.wait_ge(SA.h, sa0 + 2 * t + 1)
                    op = ve_stats_b(vector, par)
                    op.then_inc(VA.h, 1)
                    VA.inc()
                    for g in range(4):
                        vector.wait_ge(TA.h, ta0 + 4 * t + g + 1)
                        src = ppAs[g % 2][:]
                        dst = ln1_fm.rearrange("p (c f) -> p c f", c=OC)[
                            :, 4 * g:4 * g + 4, t * 128:(t + 1) * 128]
                        op = vector.tensor_copy(out=dst, in_=src.rearrange(
                            "p (c f) -> p c f", c=4))
                        op.then_inc(VB.h, 1)
                        VB.inc()
                ve_fence(vector, [(TA, ta0 + 4 * NT)])

            @block.scalar
            def _(scalar):
                barrier_start(scalar)
                for t in range(NT):
                    par = t % 2
                    xt = xin[:, par * C:(par + 1) * C]
                    scalar.wait_ge(LS[par].h, ls0[par] + 16 * (t // 2 + 1))
                    if t >= 2:
                        scalar.wait_ge(VA.h, va0 + t - 1)
                    op = sc_stats_sq(scalar, xt, par)
                    op.then_inc(SA.h, 1)
                    SA.inc()
                    scalar.wait_ge(VA.h, va0 + t + 1)
                    if t >= 2:
                        scalar.wait_ge(TA.h, ta0 + 4 * t - 4)
                    op = sc_stats_norm(scalar, xt,
                                       ln1_tm[:, par * C:(par + 1) * C], par)
                    op.then_inc(SA.h, 1)
                    SA.inc()
                    if DEV_DBG2:
                        scalar.activation(out=sdbg[:, 4 * t:4 * t + 1],
                                          in_=st_mean[:, 4 * par:4 * par + 1], func=AF.Identity)
                        scalar.activation(out=sdbg[:, 4 * t + 1:4 * t + 2],
                                          in_=st_rstd[:, 4 * par:4 * par + 1], func=AF.Identity)
                        scalar.activation(out=sdbg[:, 4 * t + 2:4 * t + 3],
                                          in_=st_negmr[:, 4 * par:4 * par + 1], func=AF.Identity)
                        scalar.activation(out=sdbg[:, 4 * t + 3:4 * t + 4],
                                          in_=st_sum[:, 4 * par:4 * par + 1], func=AF.Identity)
                if DEV_DBG2:
                    scalar.wait_ge(SA.h, SA.v)
                    scalar.dma_start(out=y_scr[0, :, 0:64],
                                     in_=sdbg[:]).then_inc(SO.h, 16)
                    SO.inc(16)
                    scalar.wait_ge(SO.h, SO.v)
                sc_fence(scalar)

            @block.tensor
            def _(tensor):
                barrier_start(tensor)
                for t in range(NT):
                    par = t % 2
                    tensor.wait_ge(SA.h, sa0 + 2 * t + 2)
                    for g in range(4):
                        if 4 * t + g >= 2:
                            tensor.wait_ge(VB.h, vb0 + 4 * t + g - 1)
                        for j in range(4):
                            cidx = 4 * g + j
                            op = tensor.transpose(
                                out=ppAs[g % 2][:, j * 128:(j + 1) * 128],
                                in_=ln1_tm[:, par * C + cidx * 128:par * C + (cidx + 1) * 128],
                                identity=sidb[:])
                        op.then_inc(TA.h, 1)
                        TA.inc()

            aes.close()
            phase_done(None)

        if want("DBGLN"):
            @block.sync
            def _(sync):
                barrier_start(sync)
                sync.nop().then_inc(PH.h, 1)

            @block.tensor
            def _(tensor):
                barrier_start(tensor)

            @block.vector
            def _(vector):
                barrier_start(vector)
                ve_fence(vector, [])

            @block.scalar
            def _(scalar):
                barrier_start(scalar)
                for ch in range(OC):
                    scalar.dma_start(out=attn_scr[ch],
                                     in_=ln1_fm[:, ch * T:(ch + 1) * T]).then_inc(SO.h, 16)
                    SO.inc(16)
                store_wait_all(scalar)
                sc_fence(scalar)

            phase_done(None)

        vtmes = contextlib.ExitStack()
        v_tm = vtmes.enter_context(nc.sbuf_tensor(f"v_tm_{s}", [128, NT * C], BF16, side="right"))

        if want("BV"):
            bes = contextlib.ExitStack()
            wbuf = bes.enter_context(nc.sbuf_tensor(f"wbufv_{s}", [128, 2 * C], BF16))
            vfm = bes.enter_context(nc.sbuf_tensor(f"vfm_{s}", [128, 2 * C], BF16))
            pB0 = bes.enter_context(nc.psum_tensor(f"pBv0_{s}", [128, 1024], F32))
            pB1 = bes.enter_context(nc.psum_tensor(f"pBv1_{s}", [128, 1024], F32))
            ppB0 = bes.enter_context(nc.psum_tensor(f"ppBv0_{s}", [128, 512], BF16))
            ppB1 = bes.enter_context(nc.psum_tensor(f"ppBv1_{s}", [128, 512], BF16))
            ppBs = [ppB0, ppB1]
            pBs = [pB0, pB1]
            ta0, tb0, sa0, vb0 = TA.v, TB.v, SA.v, VB.v
            ls0 = [LS[0].v, LS[1].v]

            @block.sync
            def _(sync):
                barrier_start(sync)
                for o in range(OC):
                    if o >= 2:
                        sync.wait_ge(TA.h, ta0 + 2 * o - 2)
                    dma_ld(sync, LS[o % 2], wbuf[:, (o % 2) * C:(o % 2 + 1) * C],
                           w_b["wv"][o])
                sync.nop().then_inc(PH.h, 1)

            @block.tensor
            def _(tensor):
                barrier_start(tensor)
                for o in range(OC):
                    for half in range(2):
                        slot = (2 * o + half) % 2
                        tensor.wait_ge(LS[o % 2].h, ls0[o % 2] + 16 * (o // 2 + 1))
                        if 2 * o + half >= 2:
                            tensor.wait_ge(SA.h, sa0 + 2 * o + half - 1)
                        for c in range(CC):
                            for n in range(2):
                                op = tensor.matmul(
                                    out=pBs[slot][:, n * 512:(n + 1) * 512],
                                    lhsT=wbuf[:, (o % 2) * C + c * 128:(o % 2) * C + (c + 1) * 128],
                                    rhs=ln1_fm[:, c * T + half * 1024 + n * 512:
                                               c * T + half * 1024 + (n + 1) * 512],
                                    start=(c == 0), stop=(c == CC - 1))
                        op.then_inc(TA.h, 1)
                        TA.inc()
                    for oo in ([o - 1] if o >= 1 else []) + ([15] if o == OC - 1 else []):
                        tensor.wait_ge(SA.h, sa0 + 2 * oo + 2)
                        for g in range(4):
                            if 4 * oo + g >= 2:
                                tensor.wait_ge(VB.h, vb0 + 4 * oo + g - 1)
                            for j in range(4):
                                tcidx = 4 * g + j
                                op = tensor.transpose(
                                    out=ppBs[g % 2][:, j * 128:(j + 1) * 128],
                                    in_=vfm[:, (oo % 2) * C + tcidx * 128:
                                            (oo % 2) * C + (tcidx + 1) * 128],
                                    identity=sidb[:])
                            op.then_inc(TB.h, 1)
                            TB.inc()

            @block.scalar
            def _(scalar):
                barrier_start(scalar)
                for o in range(OC):
                    for half in range(2):
                        scalar.wait_ge(TA.h, ta0 + 2 * o + half + 1)
                        if o >= 2 and half == 0:
                            scalar.wait_ge(TB.h, tb0 + 4 * o - 4)
                        slot = (2 * o + half) % 2
                        op = scalar.activation(
                            out=vfm[:, (o % 2) * C + half * 1024:(o % 2) * C + half * 1024 + 1024],
                            in_=pBs[slot][:], func=AF.Identity,
                            bias=sb_bias["bv"][:, o:o + 1])
                        op.then_inc(SA.h, 1)
                        SA.inc()
                sc_fence(scalar)

            @block.vector
            def _(vector):
                barrier_start(vector)
                for o in range(OC):
                    for g in range(4):
                        vector.wait_ge(TB.h, tb0 + 4 * o + g + 1)
                        src = ppBs[g % 2][:]
                        dst = v_tm.rearrange("p (tc f) -> p tc f", tc=NT)[
                            :, 4 * g:4 * g + 4, o * 128:(o + 1) * 128]
                        op = vector.tensor_copy(out=dst,
                                                in_=src.rearrange("p (c f) -> p c f", c=4))
                        op.then_inc(VB.h, 1)
                        VB.inc()
                ve_fence(vector, [(TA, ta0 + 2 * OC), (TB, tb0 + 4 * OC)])

            bes.close()
            phase_done(None)

        for pname, wsrc, bnm, dscr in (("BK", "wk", "bk", k_scr), ("BQ", "wq", "bq", q_scr)):
            if not want(pname):
                continue
            bes = contextlib.ExitStack()
            wbuf = bes.enter_context(nc.sbuf_tensor(f"wbuf{pname}_{s}", [128, 2 * C], BF16))
            ksb = bes.enter_context(nc.sbuf_tensor(f"ksb{pname}_{s}", [128, 2 * C], BF16))
            kq = bes.enter_context(nc.sbuf_tensor(f"kq{pname}_{s}", [128, 2 * C], BF16))
            t1b = bes.enter_context(nc.sbuf_tensor(f"t1{pname}_{s}", [128, 1024], F32))
            t2b = bes.enter_context(nc.sbuf_tensor(f"t2{pname}_{s}", [128, 1024], F32))
            scos = bes.enter_context(nc.sbuf_tensor(f"scos{pname}_{s}", [128, T], F32))
            ssin = bes.enter_context(nc.sbuf_tensor(f"ssin{pname}_{s}", [128, T], F32))
            pB0 = bes.enter_context(nc.psum_tensor(f"pB0{pname}_{s}", [128, 1024], F32))
            pB1 = bes.enter_context(nc.psum_tensor(f"pB1{pname}_{s}", [128, 1024], F32))
            pR0 = bes.enter_context(nc.psum_tensor(f"pR0{pname}_{s}", [128, 1024], F32))
            pR1 = bes.enter_context(nc.psum_tensor(f"pR1{pname}_{s}", [128, 1024], F32))
            pBs, pRs = [pB0, pB1], [pR0, pR1]
            ta0, sa0, va0, lo0 = TA.v, SA.v, VA.v, LO.v
            ls0 = [LS[0].v, LS[1].v]
            ss0 = [SS[0].v, SS[1].v]

            @block.sync
            def _(sync):
                barrier_start(sync)
                dma_ld(sync, LO, scos[:], cosf)
                dma_ld(sync, LO, ssin[:], sinf)
                for o in range(OC):
                    if o >= 2:
                        sync.wait_ge(TA.h, ta0 + 4 * o - 6)
                    dma_ld(sync, LS[o % 2], wbuf[:, (o % 2) * C:(o % 2 + 1) * C],
                           w_b[wsrc][o])
                sync.nop().then_inc(PH.h, 1)

            @block.tensor
            def _(tensor):
                barrier_start(tensor)
                for o in range(OC):
                    for half in range(2):
                        tensor.wait_ge(LS[o % 2].h, ls0[o % 2] + 16 * (o // 2 + 1))
                        if o >= 1:
                            tensor.wait_ge(SA.h, sa0 + 2 * (o - 1) + half + 1)
                        for c in range(CC):
                            for n in range(2):
                                op = tensor.matmul(
                                    out=pBs[half][:, n * 512:(n + 1) * 512],
                                    lhsT=wbuf[:, (o % 2) * C + c * 128:(o % 2) * C + (c + 1) * 128],
                                    rhs=ln1_fm[:, c * T + half * 1024 + n * 512:
                                               c * T + half * 1024 + (n + 1) * 512],
                                    start=(c == 0), stop=(c == CC - 1))
                        op.then_inc(TA.h, 1)
                        TA.inc()
                    for half in range(2):
                        tensor.wait_ge(SA.h, sa0 + 2 * o + half + 1)
                        if o >= 1:
                            tensor.wait_ge(VA.h, va0 + 2 * (o - 1) + half + 1)
                        for n in range(2):
                            op = tensor.matmul(
                                out=pRs[half][:, n * 512:(n + 1) * 512],
                                lhsT=srp[:],
                                rhs=ksb[:, (o % 2) * C + half * 1024 + n * 512:
                                        (o % 2) * C + half * 1024 + (n + 1) * 512],
                                start=True, stop=True)
                        op.then_inc(TA.h, 1)
                        TA.inc()

            @block.scalar
            def _(scalar):
                barrier_start(scalar)
                for o in range(OC):
                    for half in range(2):
                        scalar.wait_ge(TA.h, ta0 + 4 * o + half + 1)
                        if o >= 2 and half == 0:
                            scalar.wait_ge(VA.h, va0 + 2 * o - 2)
                        op = scalar.activation(
                            out=ksb[:, (o % 2) * C + half * 1024:(o % 2) * C + half * 1024 + 1024],
                            in_=pBs[half][:], func=AF.Identity,
                            bias=sb_bias[bnm][:, o:o + 1])
                        op.then_inc(SA.h, 1)
                        SA.inc()
                    if o >= 1:
                        scalar.wait_ge(VA.h, va0 + 2 * o)
                        scalar.dma_start(out=dscr[o - 1],
                                         in_=kq[:, ((o - 1) % 2) * C:((o - 1) % 2 + 1) * C]
                                         ).then_inc(SS[(o - 1) % 2].h, 16)
                        SS[(o - 1) % 2].inc(16)
                scalar.wait_ge(VA.h, va0 + 32)
                scalar.dma_start(out=dscr[OC - 1],
                                 in_=kq[:, ((OC - 1) % 2) * C:((OC - 1) % 2 + 1) * C]
                                 ).then_inc(SS[(OC - 1) % 2].h, 16)
                SS[(OC - 1) % 2].inc(16)
                store_wait_all(scalar)
                sc_fence(scalar)

            @block.vector
            def _(vector):
                barrier_start(vector)
                vector.wait_ge(LO.h, lo0 + 32)
                for o in range(OC):
                    for half in range(2):
                        vector.wait_ge(TA.h, ta0 + 4 * o + 2 + half + 1)
                        if o >= 2 and half == 0:
                            vector.wait_ge(SS[o % 2].h, ss0[o % 2] + 16 * ((o - 2) // 2 + 1))
                        hc = half * 1024
                        vector.tensor_tensor(out=t1b[:],
                                             in0=ksb[:, (o % 2) * C + hc:(o % 2) * C + hc + 1024],
                                             in1=scos[:, hc:hc + 1024], op=ALU.mult)
                        vector.tensor_tensor(out=t2b[:], in0=pRs[half][:],
                                             in1=ssin[:, hc:hc + 1024], op=ALU.mult)
                        op = vector.tensor_tensor(
                            out=kq[:, (o % 2) * C + hc:(o % 2) * C + hc + 1024],
                            in0=t1b[:], in1=t2b[:], op=ALU.add)
                        op.then_inc(VA.h, 1)
                        VA.inc()
                ve_fence(vector, [(TA, ta0 + 4 * OC)])

            bes.close()
            phase_done(None)

        ln1es.close()

        if want("C"):
            ces = contextlib.ExitStack()
            kh = ces.enter_context(nc.sbuf_tensor(f"kh_{s}", [128, T], BF16))
            qh = ces.enter_context(nc.sbuf_tensor(f"qh_{s}", [128, T], BF16))
            pP = ces.enter_context(nc.sbuf_tensor(f"pP_{s}", [128, 3 * T], BF16))
            ptsb = ces.enter_context(nc.sbuf_tensor(f"ptsb_{s}", [128, 1024], BF16))
            asb = ces.enter_context(nc.sbuf_tensor(f"asb_{s}", [64, 2 * T], BF16))
            pS0 = ces.enter_context(nc.psum_tensor(f"pS0_{s}", [128, 1024], F32))
            pS1 = ces.enter_context(nc.psum_tensor(f"pS1_{s}", [128, 1024], F32))
            pS2 = ces.enter_context(nc.psum_tensor(f"pS2_{s}", [128, 1024], F32))
            pPk = ces.enter_context(nc.psum_tensor(f"pPk_{s}", [128, 512], BF16))
            pO = ces.enter_context(nc.psum_tensor(f"pO_{s}", [64, 512], F32))
            pSs = [pS0, pS1, pS2]

            HL = DEV_C_HEADS if DEV_C_HEADS is not None else H
            ta0, tb0, tc0 = TA.v, TB.v, TC.v
            va0, vb0, vd0 = VA.v, VB.v, VD.v
            vc0 = VC.v
            sa0, sb0, sd0, se0 = SA.v, SB.v, SD.v, SE.v
            ls0 = [LS[0].v, LS[1].v]
            ss0 = [SS[0].v, SS[1].v]

            ev = {}
            bctr = 0
            qctr = 0
            gctr = 0
            ectr = 0
            dctr = 0
            pvg = 0
            for h in range(HL):
                ev[("load", h)] = ls0[h % 2] + 32 * (h // 2 + 1)
                for qt in range(NT):
                    klen = (qt + 1) * 128
                    nb = (klen + 1023) // 1024
                    for kb in range(nb):
                        bctr += 1
                        ev[("Smm", h, qt, kb)] = ta0 + bctr
                        ev[("red", h, qt, kb)] = va0 + bctr
                        ev[("exp", h, qt, kb)] = sa0 + bctr
                        ev[("slot", bctr - 1)] = (h, qt, kb)
                    ev[("nb", h, qt)] = nb
                    if qt >= 8:
                        dctr += 1
                        ev[("cmb1", h, qt)] = vd0 + dctr
                        ectr += 1
                        ev[("cmbE", h, qt)] = se0 + ectr
                    qctr += 1
                    ev[("cmb2", h, qt)] = vb0 + qctr
                    for kb in range(nb):
                        ev[("pn", h, qt, kb)] = vc0 + (qctr - 1) * 2 + kb + 1
                    ngrp = (qt + 1 + 3) // 4
                    for g in range(ngrp):
                        gctr += 1
                        ev[("trans", h, qt, g)] = tb0 + gctr
                        ev[("ptev", h, qt, g)] = sb0 + gctr
                        pvg += 1
                        ev[("pv", h, qt, g)] = tc0 + pvg
                    ev[("ngrp", h, qt)] = ngrp
                    ev[("oev", h, qt)] = sd0 + qctr
                ev[("store", h)] = ss0[h % 2] + 16 * (h // 2 + 1)
            n_blocks_total = bctr
            vcc = 0
            for h in range(HL):
                for qt in range(NT):
                    for kb in range(ev[("nb", h, qt)]):
                        vcc += 1
                        ev[("pn", h, qt, kb)] = vc0 + vcc

            hq_list = [(h, qt) for h in range(HL) for qt in range(NT)]
            blk_list = []
            for h in range(HL):
                for qt in range(NT):
                    for kb in range(ev[("nb", h, qt)]):
                        blk_list.append((h, qt, kb))
            grp_list = []
            for h in range(HL):
                for qt in range(NT):
                    for g in range(ev[("ngrp", h, qt)]):
                        grp_list.append((h, qt, g))

            @block.sync
            def _(sync):
                barrier_start(sync)
                for h in range(HL):
                    if h >= 2:
                        lastb = ev[("Smm", h - 2, NT - 1, ev[("nb", h - 2, NT - 1)] - 1)]
                        sync.wait_ge(TA.h, lastb)
                    rb = (h % 2) * 64
                    dma_ld(sync, LS[h % 2], kh[rb:rb + 64, :], k_scr[h // 2, rb:rb + 64, :])
                    dma_ld(sync, LS[h % 2], qh[rb:rb + 64, :], q_scr[h // 2, rb:rb + 64, :])
                sync.nop().then_inc(PH.h, 1)

            @block.tensor
            def _(tensor):
                barrier_start(tensor)
                bi = 0
                gi = 0
                for h in range(HL):
                    rb = (h % 2) * 64
                    for qt in range(NT):
                        klen = (qt + 1) * 128
                        nb = ev[("nb", h, qt)]
                        for kb in range(nb):
                            cw = min(1024, klen - kb * 1024)
                            slot = bi % 3
                            tensor.wait_ge(LS[h % 2].h, ev[("load", h)])
                            if bi >= 3:
                                oh, oqt, okb = ev[("slot", bi - 3)]
                                tensor.wait_ge(SA.h, ev[("exp", oh, oqt, okb)])
                            for n in range((cw + 511) // 512):
                                w = min(512, cw - n * 512)
                                op = tensor.matmul(
                                    out=pSs[slot][:, n * 512:n * 512 + w],
                                    lhsT=qh[rb:rb + 64, qt * 128:(qt + 1) * 128],
                                    rhs=kh[rb:rb + 64, kb * 1024 + n * 512:kb * 1024 + n * 512 + w],
                                    start=True, stop=True)
                            op.then_inc(TA.h, 1)
                            TA.inc()
                            bi += 1
                        for pqt in ([qt - 1] if qt >= 1 else []) + ([NT - 1] if qt == NT - 1 else []):
                            pklen = (pqt + 1) * 128
                            pkt_n = pqt + 1
                            pngrp = ev[("ngrp", h, pqt)]
                            pbuf = ((h * NT + pqt) % 3)
                            for g in range(pngrp):
                                kts = list(range(4 * g, min(4 * g + 4, pkt_n)))
                                kb_hi = (kts[-1] * 128) // 1024
                                tensor.wait_ge(VC.h, ev[("pn", h, pqt, kb_hi)])
                                if gi >= 1:
                                    ph_, pq_, pg_ = grp_list[gi - 1]
                                    tensor.wait_ge(SB.h, ev[("ptev", ph_, pq_, pg_)])
                                for j, kt in enumerate(kts):
                                    op = tensor.transpose(
                                        out=pPk[:, j * 128:(j + 1) * 128],
                                        in_=pP[:, pbuf * T + kt * 128:pbuf * T + (kt + 1) * 128],
                                        identity=sidb[:])
                                op.then_inc(TB.h, 1)
                                TB.inc()
                                tensor.wait_ge(SB.h, ev[("ptev", h, pqt, g)])
                                if g == 0 and (h * NT + pqt) >= 1:
                                    oh_, oq_ = hq_list[h * NT + pqt - 1]
                                    tensor.wait_ge(SD.h, ev[("oev", oh_, oq_)])
                                for j, kt in enumerate(kts):
                                    op = tensor.matmul(
                                        out=pO[:, (pqt % 4) * 128:(pqt % 4 + 1) * 128],
                                        lhsT=v_tm[:, kt * C + h * 64:kt * C + h * 64 + 64],
                                        rhs=ptsb[:, (g % 2) * 512 + j * 128:(g % 2) * 512 + (j + 1) * 128],
                                        start=(kt == 0), stop=(kt == pkt_n - 1))
                                op.then_inc(TC.h, 1)
                                TC.inc()
                                gi += 1

            @block.vector
            def _(vector):
                barrier_start(vector)
                bi = 0
                for h in range(HL):
                    for qt in range(NT):
                        klen = (qt + 1) * 128
                        nb = ev[("nb", h, qt)]
                        qg = h * NT + qt
                        pbuf = qg % 3
                        sc0 = pbuf * 8
                        for kb in range(nb):
                            cw = min(1024, klen - kb * 1024)
                            slot = bi % 3
                            vector.wait_ge(TA.h, ev[("Smm", h, qt, kb)])
                            if kb == nb - 1:
                                doff = klen - kb * 1024 - 128
                                vector.tensor_tensor(
                                    out=pSs[slot][:, doff:doff + 128],
                                    in0=pSs[slot][:, doff:doff + 128],
                                    in1=smask[:], op=ALU.add)
                            op = vector.tensor_reduce(
                                out=c_mkb[:, sc0 + 4 * kb:sc0 + 4 * kb + 1],
                                in_=pSs[slot][:, 0:cw], axis=AX.X, op=ALU.max,
                                negate=True)
                            op.then_inc(VA.h, 1)
                            VA.inc()
                            bi += 1
                        vector.wait_ge(SA.h, ev[("exp", h, qt, nb - 1)])
                        if qt >= 8:
                            vector.wait_ge(VA.h, VA.v)
                            op = vector.tensor_tensor(out=c_negmq[:, 4 * pbuf:4 * pbuf + 1],
                                                 in0=c_mkb[:, sc0:sc0 + 1],
                                                 in1=c_mkb[:, sc0 + 4:sc0 + 5], op=ALU.min)
                            op.then_inc(VF.h, 1)
                            VF.inc()
                            vector.wait_ge(VF.h, VF.v)
                            vector.tensor_tensor(out=c_d[:, sc0:sc0 + 1],
                                                 in0=c_negmq[:, 4 * pbuf:4 * pbuf + 1],
                                                 in1=c_mkb[:, sc0:sc0 + 1], op=ALU.subtract)
                            op = vector.tensor_tensor(out=c_d[:, sc0 + 4:sc0 + 5],
                                                      in0=c_negmq[:, 4 * pbuf:4 * pbuf + 1],
                                                      in1=c_mkb[:, sc0 + 4:sc0 + 5],
                                                      op=ALU.subtract)
                            op.then_inc(VD.h, 1)
                            VD.inc()
                            vector.wait_ge(SE.h, ev[("cmbE", h, qt)])
                            op = vector.tensor_tensor(out=c_rE[:, sc0:sc0 + 8:4],
                                                 in0=c_rkb[:, sc0:sc0 + 8:4],
                                                 in1=c_e[:, sc0:sc0 + 8:4], op=ALU.mult)
                            op.then_inc(VF.h, 1)
                            VF.inc()
                            vector.wait_ge(VF.h, VF.v)
                            op = vector.tensor_tensor(out=c_rq[:, 4 * pbuf:4 * pbuf + 1],
                                                 in0=c_rE[:, sc0:sc0 + 1],
                                                 in1=c_rE[:, sc0 + 4:sc0 + 5], op=ALU.add)
                            op.then_inc(VF.h, 1)
                            VF.inc()
                            vector.wait_ge(VF.h, VF.v)
                            op = vector.reciprocal(out=c_rq[:, 4 * pbuf:4 * pbuf + 1],
                                              in_=c_rq[:, 4 * pbuf:4 * pbuf + 1])
                            op.then_inc(VF.h, 1)
                            VF.inc()
                            vector.wait_ge(VF.h, VF.v)
                            op = vector.tensor_scalar(out=c_scl[:, sc0:sc0 + 8:4],
                                                      in0=c_e[:, sc0:sc0 + 8:4],
                                                      scalar1=c_rq[:, 4 * pbuf:4 * pbuf + 1],
                                                      scalar2=None, op0=ALU.mult)
                        else:
                            vector.wait_ge(SA.h, ev[("exp", h, qt, nb - 1)])
                            op = vector.reciprocal(out=c_scl[:, sc0:sc0 + 1],
                                                   in_=c_rkb[:, sc0:sc0 + 1])
                        op.then_inc(VB.h, 1)
                        VB.inc()
                        vector.wait_ge(VB.h, VB.v)
                        for kb in range(nb):
                            cw = min(1024, klen - kb * 1024)
                            op = vector.tensor_scalar(
                                out=pP[:, pbuf * T + kb * 1024:pbuf * T + kb * 1024 + cw],
                                in0=pP[:, pbuf * T + kb * 1024:pbuf * T + kb * 1024 + cw],
                                scalar1=c_scl[:, sc0 + 4 * kb:sc0 + 4 * kb + 1],
                                scalar2=None, op0=ALU.mult)
                            op.then_inc(VC.h, 1)
                            VC.inc()
                ve_fence(vector, [(TA, ta0 + n_blocks_total),
                                  (TB, tb0 + gctr), (TC, tc0 + pvg)])

            @block.scalar
            def _(scalar):
                barrier_start(scalar)
                bi = 0
                gi = 0
                for h in range(HL):
                    rb = (h % 2) * 64
                    for qt in range(NT):
                        klen = (qt + 1) * 128
                        nb = ev[("nb", h, qt)]
                        qg = h * NT + qt
                        pbuf = qg % 3
                        sc0 = pbuf * 8
                        for kb in range(nb):
                            cw = min(1024, klen - kb * 1024)
                            slot = bi % 3
                            scalar.wait_ge(VA.h, ev[("red", h, qt, kb)])
                            if qg >= 3 and kb == 0:
                                oh_, oq_ = hq_list[qg - 3]
                                scalar.wait_ge(TB.h, ev[("trans", oh_, oq_,
                                                          ev[("ngrp", oh_, oq_)] - 1)])
                            scalar.activation(
                                out=pP[:, pbuf * T + kb * 1024:pbuf * T + kb * 1024 + cw],
                                in_=pSs[slot][:, 0:cw], func=AF.Exp,
                                bias=c_mkb[:, sc0 + 4 * kb:sc0 + 4 * kb + 1],
                                accum_out=c_rkb[:, sc0 + 4 * kb:sc0 + 4 * kb + 1])
                            op = scalar.activation(out=fS[:],
                                                   in_=c_rkb[:, sc0 + 4 * kb:sc0 + 4 * kb + 1],
                                                   func=AF.Identity)
                            op.then_inc(SA.h, 1)
                            SA.inc()
                            bi += 1
                        if qt >= 8:
                            scalar.wait_ge(VD.h, ev[("cmb1", h, qt)])
                            op = scalar.activation(out=c_e[:, sc0:sc0 + 8:4],
                                                   in_=c_d[:, sc0:sc0 + 8:4], func=AF.Exp)
                            op.then_inc(SE.h, 1)
                            SE.inc()
                        for pqt in ([qt - 1] if qt >= 1 else []) + ([NT - 1] if qt == NT - 1 else []):
                            pkt_n = pqt + 1
                            pngrp = ev[("ngrp", h, pqt)]
                            for g in range(pngrp):
                                cnt = min(4, pkt_n - 4 * g)
                                scalar.wait_ge(TB.h, ev[("trans", h, pqt, g)])
                                if gi >= 2:
                                    p2 = grp_list[gi - 2]
                                    scalar.wait_ge(TC.h, ev[("pv", p2[0], p2[1], p2[2])])
                                op = scalar.copy(
                                    out=ptsb[:, (g % 2) * 512:(g % 2) * 512 + cnt * 128],
                                    in_=pPk[:, 0:cnt * 128])
                                op.then_inc(SB.h, 1)
                                SB.inc()
                                gi += 1
                            scalar.wait_ge(TC.h, ev[("pv", h, pqt, pngrp - 1)])
                            if h >= 2 and pqt == 0:
                                scalar.wait_ge(SS[h % 2].h, ev[("store", h - 2)])
                            op = scalar.activation(
                                out=asb[:, (h % 2) * T + pqt * 128:(h % 2) * T + (pqt + 1) * 128],
                                in_=pO[:, (pqt % 4) * 128:(pqt % 4 + 1) * 128],
                                func=AF.Identity)
                            op.then_inc(SD.h, 1)
                            SD.inc()
                    rb = (h % 2) * 64
                    scalar.dma_start(out=attn_scr[h // 2, rb:rb + 64, :],
                                     in_=asb[:, (h % 2) * T:(h % 2 + 1) * T]
                                     ).then_inc(SS[h % 2].h, 16)
                    SS[h % 2].inc(16)
                store_wait_all(scalar)
                sc_fence(scalar)

            ces.close()
            phase_done(None)

        vtmes.close()
        seq_es2 = contextlib.ExitStack()

        if want("D"):
            des = contextlib.ExitStack()
            attn_fm = des.enter_context(nc.sbuf_tensor(f"attnfm_{s}", [128, OC * T], BF16))
            wbuf = des.enter_context(nc.sbuf_tensor(f"wbufD_{s}", [128, 2 * C], BF16))
            sofm = des.enter_context(nc.sbuf_tensor(f"sofmD_{s}", [128, 2 * 1024], F32))
            out_tm = des.enter_context(nc.sbuf_tensor(f"outtmD_{s}", [128, 8 * C], F32))
            pD0 = des.enter_context(nc.psum_tensor(f"pD0_{s}", [128, 1024], F32))
            pD1 = des.enter_context(nc.psum_tensor(f"pD1_{s}", [128, 1024], F32))
            pPkf = des.enter_context(nc.psum_tensor(f"pPkf_{s}", [128, 1024], F32))
            pDs = [pD0, pD1]

            ta0, tb0, sa0, sb0, va0, vb0, vd0, sd0, lo0, so0, vf0 = (
                TA.v, TB.v, SA.v, SB.v, VA.v, VB.v, VD.v, SD.v, LO.v, SO.v, VF.v)
            ls0 = [LS[0].v, LS[1].v]
            lx0 = [LX[0].v, LX[1].v]

            @block.sync
            def _(sync):
                barrier_start(sync)
                for ch in range(OC):
                    dma_ld(sync, LO, attn_fm[:, ch * T:(ch + 1) * T], attn_scr[ch])
                for hf in range(2):
                    for o in range(OC):
                        oo = hf * OC + o
                        if oo >= 2:
                            sync.wait_ge(TA.h, ta0 + oo - 1)
                        dma_ld(sync, LS[oo % 2], wbuf[:, (oo % 2) * C:(oo % 2 + 1) * C],
                               w_b["wo"][o])
                    for tc in range(8):
                        tg = hf * 8 + tc
                        if tg >= 2:
                            sync.wait_ge(VD.h, vd0 + tg - 1)
                        dma_ld(sync, LX[tg % 2], xin[:, (tg % 2) * C:(tg % 2 + 1) * C],
                               x_in[s, tg * 128:(tg + 1) * 128, :])
                sync.nop().then_inc(PH.h, 1)

            @block.tensor
            def _(tensor):
                barrier_start(tensor)
                gi = 0
                def d_trans(po):
                    nonlocal gi
                    tensor.wait_ge(SA.h, sa0 + po + 1)
                    for g in range(2):
                        if gi >= 2:
                            tensor.wait_ge(VB.h, vb0 + gi - 1)
                        for j in range(4):
                            tcix = 4 * g + j
                            op = tensor.transpose(
                                out=pPkf[:, (g % 2) * 512 + j * 128:(g % 2) * 512 + (j + 1) * 128],
                                in_=sofm[:, (po % 2) * 1024 + tcix * 128:
                                         (po % 2) * 1024 + (tcix + 1) * 128],
                                identity=sidf[:])
                        op.then_inc(TB.h, 1)
                        TB.inc()
                        gi += 1
                for hf in range(2):
                    for o in range(OC):
                        oo = hf * OC + o
                        slot = oo % 2
                        if oo >= 1:
                            d_trans(oo - 1)
                        if oo == 0:
                            tensor.wait_ge(LO.h, lo0 + 16 * OC)
                        tensor.wait_ge(LS[oo % 2].h, ls0[oo % 2] + 16 * (oo // 2 + 1))
                        if oo >= 2:
                            tensor.wait_ge(SA.h, sa0 + oo - 1)
                        for c in range(CC):
                            for n in range(2):
                                op = tensor.matmul(
                                    out=pDs[slot][:, n * 512:(n + 1) * 512],
                                    lhsT=wbuf[:, slot * C + c * 128:slot * C + (c + 1) * 128],
                                    rhs=attn_fm[:, c * T + hf * 1024 + n * 512:
                                                c * T + hf * 1024 + (n + 1) * 512],
                                    start=(c == 0), stop=(c == CC - 1))
                        op.then_inc(TA.h, 1)
                        TA.inc()
                d_trans(31)

            @block.scalar
            def _(scalar):
                barrier_start(scalar)
                for hf in range(2):
                    for o in range(OC):
                        oo = hf * OC + o
                        slot = oo % 2
                        scalar.wait_ge(TA.h, ta0 + oo + 1)
                        if oo >= 2:
                            scalar.wait_ge(TB.h, tb0 + 2 * oo - 2)
                        op = scalar.activation(out=sofm[:, slot * 1024:(slot + 1) * 1024],
                                               in_=pDs[slot][:], func=AF.Identity,
                                               bias=sb_bias["bo"][:, o:o + 1])
                        op.then_inc(SA.h, 1)
                        SA.inc()
                    for tc in range(8):
                        tg = hf * 8 + tc
                        par = tg % 2
                        src = out_tm[:, tc * C:(tc + 1) * C]
                        scalar.wait_ge(VF.h, vf0 + 2 * tg + 1)
                        op = sc_stats_sq(scalar, src, par)
                        op.then_inc(SB.h, 1)
                        SB.inc()
                        scalar.wait_ge(VA.h, va0 + tg + 1)
                        op = sc_stats_norm(scalar, src, src, par)
                        op.then_inc(SD.h, 1)
                        SD.inc()
                    for tc in range(8):
                        tg = hf * 8 + tc
                        scalar.wait_ge(VD.h, vd0 + tg + 1)
                        scalar.dma_start(out=y_scr[tg],
                                         in_=out_tm[:, tc * C:(tc + 1) * C]
                                         ).then_inc(SO.h, 16)
                        SO.inc(16)
                store_wait_all(scalar)
                sc_fence(scalar)

            @block.vector
            def _(vector):
                barrier_start(vector)
                gi = 0
                for hf in range(2):
                    for oo_l in range(OC):
                        for g in range(2):
                            vector.wait_ge(TB.h, tb0 + gi + 1)
                            if hf == 1 and oo_l == 0 and g == 0:
                                vector.wait_ge(SO.h, so0 + 16 * 8)
                            src = pPkf[:, (g % 2) * 512:(g % 2) * 512 + 512]
                            dst = out_tm.rearrange("p (tc f) -> p tc f", tc=8)[
                                :, 4 * g:4 * g + 4, oo_l * 128:(oo_l + 1) * 128]
                            op = vector.tensor_copy(out=dst,
                                                    in_=src.rearrange("p (c f) -> p c f", c=4))
                            op.then_inc(VB.h, 1)
                            VB.inc()
                            gi += 1
                    for tc in range(8):
                        tg = hf * 8 + tc
                        par = tg % 2
                        src = out_tm[:, tc * C:(tc + 1) * C]
                        if tc == 0:
                            vector.wait_ge(TB.h, tb0 + 2 * (hf * OC + OC))
                        if tg >= 2:
                            vector.wait_ge(SD.h, sd0 + tg - 1)
                        ve_stats_a(vector, src, par)
                        vector.wait_ge(SB.h, sb0 + tg + 1)
                        op = ve_stats_b(vector, par)
                        op.then_inc(VA.h, 1)
                        VA.inc()
                        vector.wait_ge(SD.h, sd0 + tg + 1)
                        vector.wait_ge(LX[tg % 2].h, lx0[tg % 2] + 16 * (tg // 2 + 1))
                        op = vector.tensor_tensor(
                            out=src, in0=src,
                            in1=xin[:, par * C:(par + 1) * C], op=ALU.add)
                        op.then_inc(VD.h, 1)
                        VD.inc()
                ve_fence(vector, [(TA, ta0 + 2 * OC), (TB, tb0 + 4 * OC)])

            des.close()
            phase_done(None)

        for hf in range(2):
            if not want(f"E{hf}"):
                continue
            ees = contextlib.ExitStack()
            ln2es = contextlib.ExitStack()
            ln2_fm = ln2es.enter_context(nc.sbuf_tensor(f"ln2fm_{s}_{hf}", [128, OC * 1024], BF16, side="right"))
            h_fm = ees.enter_context(nc.sbuf_tensor(f"hfm_{s}_{hf}", [128, ICN * 1024], BF16))

            e1s = contextlib.ExitStack()
            ln2tm = e1s.enter_context(nc.sbuf_tensor(f"ln2tm_{s}_{hf}", [128, 2 * C], BF16))
            pPkE0 = e1s.enter_context(nc.psum_tensor(f"pPkE0_{s}_{hf}", [128, 512], BF16))
            pPkE1 = e1s.enter_context(nc.psum_tensor(f"pPkE1_{s}_{hf}", [128, 512], BF16))
            pPkEs = [pPkE0, pPkE1]
            va0, sa0, sb0, ta0, vb0 = VA.v, SA.v, SB.v, TA.v, VB.v
            ls0 = [LS[0].v, LS[1].v]

            @block.sync
            def _(sync):
                barrier_start(sync)
                for tc in range(8):
                    tg = hf * 8 + tc
                    if tc >= 2:
                        sync.wait_ge(SA.h, sa0 + tc - 1)
                    dma_ld(sync, LS[tc % 2], xin[:, (tc % 2) * C:(tc % 2 + 1) * C],
                           y_scr[tg])
                sync.nop().then_inc(PH.h, 1)

            @block.vector
            def _(vector):
                barrier_start(vector)
                for tc in range(8):
                    par = tc % 2
                    src = xin[:, par * C:(par + 1) * C]
                    vector.wait_ge(LS[par].h, ls0[par] + 16 * (tc // 2 + 1))
                    if tc >= 2:
                        vector.wait_ge(SA.h, sa0 + tc - 1)
                    ve_stats_a(vector, src, par)
                    vector.wait_ge(SB.h, sb0 + tc + 1)
                    op = ve_stats_b(vector, par)
                    op.then_inc(VA.h, 1)
                    VA.inc()
                    for g in range(4):
                        vector.wait_ge(TA.h, ta0 + 4 * tc + g + 1)
                        src2 = pPkEs[g % 2][:]
                        dst = ln2_fm.rearrange("p (c f) -> p c f", c=OC)[
                            :, 4 * g:4 * g + 4, tc * 128:(tc + 1) * 128]
                        op = vector.tensor_copy(out=dst,
                                                in_=src2.rearrange("p (c f) -> p c f", c=4))
                        op.then_inc(VB.h, 1)
                        VB.inc()
                ve_fence(vector, [(TA, ta0 + 4 * 8)])

            @block.scalar
            def _(scalar):
                barrier_start(scalar)
                for tc in range(8):
                    par = tc % 2
                    src = xin[:, par * C:(par + 1) * C]
                    scalar.wait_ge(LS[par].h, ls0[par] + 16 * (tc // 2 + 1))
                    if tc >= 2:
                        scalar.wait_ge(VA.h, va0 + tc - 1)
                    op = sc_stats_sq(scalar, src, par)
                    op.then_inc(SB.h, 1)
                    SB.inc()
                    scalar.wait_ge(VA.h, va0 + tc + 1)
                    if tc >= 2:
                        scalar.wait_ge(TA.h, ta0 + 4 * tc - 4)
                    op = sc_stats_norm(scalar, src,
                                       ln2tm[:, par * C:(par + 1) * C], par)
                    op.then_inc(SA.h, 1)
                    SA.inc()
                sc_fence(scalar)

            @block.tensor
            def _(tensor):
                barrier_start(tensor)
                for tc in range(8):
                    par = tc % 2
                    tensor.wait_ge(SA.h, sa0 + tc + 1)
                    for g in range(4):
                        if 4 * tc + g >= 2:
                            tensor.wait_ge(VB.h, vb0 + 4 * tc + g - 1)
                        for j in range(4):
                            cidx = 4 * g + j
                            op = tensor.transpose(
                                out=pPkEs[g % 2][:, j * 128:(j + 1) * 128],
                                in_=ln2tm[:, par * C + cidx * 128:par * C + (cidx + 1) * 128],
                                identity=sidb[:])
                        op.then_inc(TA.h, 1)
                        TA.inc()

            e1s.close()
            phase_done(None)

            e2 = contextlib.ExitStack()
            gwb = e2.enter_context(nc.sbuf_tensor(f"gwb_{s}_{hf}", [128, 2 * C], BF16))
            uwb = e2.enter_context(nc.sbuf_tensor(f"uwb_{s}_{hf}", [128, 2 * C], BF16))
            ssg = e2.enter_context(nc.sbuf_tensor(f"ssg_{s}_{hf}", [128, 2 * 1024], F32))
            pG0 = e2.enter_context(nc.psum_tensor(f"pG0_{s}_{hf}", [128, 1024], F32))
            pG1 = e2.enter_context(nc.psum_tensor(f"pG1_{s}_{hf}", [128, 1024], F32))
            pU0 = e2.enter_context(nc.psum_tensor(f"pU0_{s}_{hf}", [128, 1024], F32))
            pU1 = e2.enter_context(nc.psum_tensor(f"pU1_{s}_{hf}", [128, 1024], F32))
            pGs, pUs = [pG0, pG1], [pU0, pU1]
            ta0, sa0, va0 = TA.v, SA.v, VA.v
            ls0 = [LS[0].v, LS[1].v]

            @block.sync
            def _(sync):
                barrier_start(sync)
                for i in range(ICN):
                    if i >= 2:
                        sync.wait_ge(TA.h, ta0 + i - 1)
                    dma_ld(sync, LS[i % 2], gwb[:, (i % 2) * C:(i % 2 + 1) * C],
                           w_b["gw"][i])
                    dma_ld(sync, LS[i % 2], uwb[:, (i % 2) * C:(i % 2 + 1) * C],
                           w_b["uw"][i])
                sync.nop().then_inc(PH.h, 1)

            @block.tensor
            def _(tensor):
                barrier_start(tensor)
                for i in range(ICN):
                    par = i % 2
                    tensor.wait_ge(LS[par].h, ls0[par] + 32 * (i // 2 + 1))
                    if i >= 2:
                        tensor.wait_ge(SA.h, sa0 + i - 1)
                        tensor.wait_ge(VA.h, va0 + i - 1)
                    for c in range(CC):
                        for n in range(2):
                            tensor.matmul(
                                out=pGs[par][:, n * 512:(n + 1) * 512],
                                lhsT=gwb[:, par * C + c * 128:par * C + (c + 1) * 128],
                                rhs=ln2_fm[:, c * 1024 + n * 512:c * 1024 + (n + 1) * 512],
                                start=(c == 0), stop=(c == CC - 1))
                    for c in range(CC):
                        for n in range(2):
                            op = tensor.matmul(
                                out=pUs[par][:, n * 512:(n + 1) * 512],
                                lhsT=uwb[:, par * C + c * 128:par * C + (c + 1) * 128],
                                rhs=ln2_fm[:, c * 1024 + n * 512:c * 1024 + (n + 1) * 512],
                                start=(c == 0), stop=(c == CC - 1))
                    op.then_inc(TA.h, 1)
                    TA.inc()

            @block.scalar
            def _(scalar):
                barrier_start(scalar)
                for i in range(ICN):
                    par = i % 2
                    scalar.wait_ge(TA.h, ta0 + i + 1)
                    if i >= 2:
                        scalar.wait_ge(VA.h, va0 + i - 1)
                    op = scalar.activation(out=ssg[:, par * 1024:(par + 1) * 1024],
                                           in_=pGs[par][:], func=AF.Silu,
                                           scale=sb_gs[:, i:i + 1])
                    op.then_inc(SA.h, 1)
                    SA.inc()
                sc_fence(scalar)

            @block.vector
            def _(vector):
                barrier_start(vector)
                for i in range(ICN):
                    par = i % 2
                    vector.wait_ge(SA.h, sa0 + i + 1)
                    op = vector.scalar_tensor_tensor(
                        out=h_fm[:, i * 1024:(i + 1) * 1024],
                        in0=pUs[par][:], scalar=sb_us[:, i:i + 1],
                        in1=ssg[:, par * 1024:(par + 1) * 1024],
                        op0=ALU.mult, op1=ALU.mult)
                    op.then_inc(VA.h, 1)
                    VA.inc()
                ve_fence(vector, [(TA, ta0 + ICN)])

            e2.close()
            ln2es.close()
            phase_done(None)

            e3 = contextlib.ExitStack()
            dwb = e3.enter_context(nc.sbuf_tensor(f"dwb_{s}_{hf}", [128, 2 * I], BF16))
            sofm = e3.enter_context(nc.sbuf_tensor(f"sofmE_{s}_{hf}", [128, 1024], F32))
            out_tm = e3.enter_context(nc.sbuf_tensor(f"outtmE_{s}_{hf}", [128, 8 * C], F32))
            pD0 = e3.enter_context(nc.psum_tensor(f"pDE0_{s}_{hf}", [128, 1024], F32))
            pD1 = e3.enter_context(nc.psum_tensor(f"pDE1_{s}_{hf}", [128, 1024], F32))
            pPkf = e3.enter_context(nc.psum_tensor(f"pPkfE_{s}_{hf}", [128, 1024], F32))
            pDs = [pD0, pD1]
            ta0, tb0, sa0, sb0, va0, vb0, vd0, sd0, vf0 = (
                TA.v, TB.v, SA.v, SB.v, VA.v, VB.v, VD.v, SD.v, VF.v)
            ls0 = [LS[0].v, LS[1].v]
            lx0 = [LX[0].v, LX[1].v]

            @block.sync
            def _(sync):
                barrier_start(sync)
                for o in range(OC):
                    if o >= 2:
                        sync.wait_ge(TA.h, ta0 + o - 1)
                    dma_ld(sync, LS[o % 2], dwb[:, (o % 2) * I:(o % 2 + 1) * I],
                           w_b["dw"][o])
                for tc in range(8):
                    tg = hf * 8 + tc
                    if tc >= 2:
                        sync.wait_ge(VD.h, vd0 + tc - 1)
                    dma_ld(sync, LX[tc % 2], xin[:, (tc % 2) * C:(tc % 2 + 1) * C],
                           y_scr[tg])
                sync.nop().then_inc(PH.h, 1)

            @block.tensor
            def _(tensor):
                barrier_start(tensor)
                gi = 0
                for o in range(OC):
                    slot = o % 2
                    tensor.wait_ge(LS[slot].h, ls0[slot] + 16 * (o // 2 + 1))
                    if o >= 2:
                        tensor.wait_ge(SA.h, sa0 + o - 1)
                    for i in range(ICN):
                        for n in range(2):
                            op = tensor.matmul(
                                out=pDs[slot][:, n * 512:(n + 1) * 512],
                                lhsT=dwb[:, slot * I + i * 128:slot * I + (i + 1) * 128],
                                rhs=h_fm[:, i * 1024 + n * 512:i * 1024 + (n + 1) * 512],
                                start=(i == 0), stop=(i == ICN - 1))
                    op.then_inc(TA.h, 1)
                    TA.inc()
                    for po in ([o - 1] if o >= 1 else []) + ([15] if o == OC - 1 else []):
                        tensor.wait_ge(SA.h, sa0 + po + 1)
                        for g in range(2):
                            if gi >= 2:
                                tensor.wait_ge(VB.h, vb0 + gi - 1)
                            for j in range(4):
                                tcix = 4 * g + j
                                op = tensor.transpose(
                                    out=pPkf[:, (g % 2) * 512 + j * 128:(g % 2) * 512 + (j + 1) * 128],
                                    in_=sofm[:, tcix * 128:(tcix + 1) * 128],
                                    identity=sidf[:])
                            op.then_inc(TB.h, 1)
                            TB.inc()
                            gi += 1

            @block.scalar
            def _(scalar):
                barrier_start(scalar)
                for o in range(OC):
                    slot = o % 2
                    scalar.wait_ge(TA.h, ta0 + o + 1)
                    if o >= 1:
                        scalar.wait_ge(TB.h, tb0 + 2 * o)
                    op = scalar.activation(out=sofm[:], in_=pDs[slot][:],
                                           func=AF.Identity, scale=sb_ds[:, o:o + 1])
                    op.then_inc(SA.h, 1)
                    SA.inc()
                for tc in range(8):
                    tg = hf * 8 + tc
                    par = tc % 2
                    src = out_tm[:, tc * C:(tc + 1) * C]
                    scalar.wait_ge(VF.h, vf0 + 2 * tc + 1)
                    op = sc_stats_sq(scalar, src, par)
                    op.then_inc(SB.h, 1)
                    SB.inc()
                    scalar.wait_ge(VA.h, va0 + tc + 1)
                    op = sc_stats_norm(scalar, src, src, par)
                    op.then_inc(SD.h, 1)
                    SD.inc()
                for tc in range(8):
                    tg = hf * 8 + tc
                    scalar.wait_ge(VD.h, vd0 + tc + 1)
                    scalar.dma_start(out=y_out[s, tg * 128:(tg + 1) * 128, :],
                                     in_=out_tm[:, tc * C:(tc + 1) * C]
                                     ).then_inc(SO.h, 16)
                    SO.inc(16)
                store_wait_all(scalar)
                sc_fence(scalar)

            @block.vector
            def _(vector):
                barrier_start(vector)
                gi = 0
                for o in range(OC):
                    for g in range(2):
                        vector.wait_ge(TB.h, tb0 + gi + 1)
                        src = pPkf[:, (g % 2) * 512:(g % 2) * 512 + 512]
                        dst = out_tm.rearrange("p (tc f) -> p tc f", tc=8)[
                            :, 4 * g:4 * g + 4, o * 128:(o + 1) * 128]
                        op = vector.tensor_copy(out=dst,
                                                in_=src.rearrange("p (c f) -> p c f", c=4))
                        op.then_inc(VB.h, 1)
                        VB.inc()
                        gi += 1
                for tc in range(8):
                    tg = hf * 8 + tc
                    par = tc % 2
                    src = out_tm[:, tc * C:(tc + 1) * C]
                    if tc == 0:
                        vector.wait_ge(TB.h, tb0 + 2 * OC)
                    if tc >= 2:
                        vector.wait_ge(SD.h, sd0 + tc - 1)
                    ve_stats_a(vector, src, par)
                    vector.wait_ge(SB.h, sb0 + tc + 1)
                    op = ve_stats_b(vector, par)
                    op.then_inc(VA.h, 1)
                    VA.inc()
                    vector.wait_ge(SD.h, sd0 + tc + 1)
                    vector.wait_ge(LX[par].h, lx0[par] + 16 * (tc // 2 + 1))
                    op = vector.tensor_tensor(
                        out=src, in0=src,
                        in1=xin[:, par * C:(par + 1) * C], op=ALU.add)
                    op.then_inc(VD.h, 1)
                    VD.inc()
                ve_fence(vector, [(TA, ta0 + OC), (TB, tb0 + 2 * OC)])

            e3.close()
            ees.close()
            phase_done(None)

        seq_es2.close()
        seq_es.close()

    es.close()
    return nc



def _tiles_wT(W):
    O, Ci = W.shape
    oc, cc = O // 128, Ci // 128
    t = W.reshape(oc, 128, cc, 128).transpose(0, 3, 2, 1)
    return np.ascontiguousarray(t.reshape(oc, 128, cc * 128), dtype=np.float32)


def _col_chunks(v):
    n = v.shape[0] // 128
    return np.ascontiguousarray(v.reshape(n, 128).T, dtype=np.float32)


def _tables():
    j = np.arange(32, dtype=np.float64)
    theta = 10000.0 ** (-2.0 * j / 64.0)
    t = np.arange(T, dtype=np.float64)
    tid = np.floor((t // 576).astype(np.float64) * 0.5)
    ang = np.outer(theta, t + tid)
    cos32 = np.cos(ang)
    sin32 = np.sin(ang)
    cosf = np.tile(cos32, (4, 1)).astype(np.float32)
    sinf = np.tile(sin32, (4, 1)).astype(np.float32)
    mask = np.triu(np.full((128, 128), -1e30, dtype=np.float32), k=1)
    rp = np.zeros((128, 128), dtype=np.float32)
    for p in range(128):
        s = -1.0 if (p % 64) < 32 else 1.0
        rp[p ^ 32, p] = s
    return cosf, sinf, mask, rp.astype(ml_dtypes.bfloat16)


_BUILD_CACHE = {}


def kernel(L0_attn_hidden_states, L0_attn_Wq, L0_attn_bq, L0_attn_Wk, L0_attn_bk,
           L0_attn_Wv, L0_attn_bv, L0_attn_Wo, L0_attn_bo,
           L0_ffn_gate_w, L0_ffn_up_w, L0_ffn_down_w,
           L0_ffn_gate_s, L0_ffn_up_s, L0_ffn_down_s):
    x = np.asarray(L0_attn_hidden_states, dtype=np.float32)
    cosf, sinf, mask, rp = _tables()
    idb = np.eye(128, dtype=ml_dtypes.bfloat16)
    idf = np.eye(128, dtype=np.float32)

    shared = dict(
        wq_t=_tiles_wT(np.asarray(L0_attn_Wq, np.float32)),
        wk_t=_tiles_wT(np.asarray(L0_attn_Wk, np.float32)),
        wv_t=_tiles_wT(np.asarray(L0_attn_Wv, np.float32)),
        wo_t=_tiles_wT(np.asarray(L0_attn_Wo, np.float32)),
        gw_t=_tiles_wT(np.asarray(L0_ffn_gate_w, np.float32)),
        uw_t=_tiles_wT(np.asarray(L0_ffn_up_w, np.float32)),
        dw_t=_tiles_wT(np.asarray(L0_ffn_down_w, np.float32)),
        bq_c=_col_chunks(np.asarray(L0_attn_bq, np.float32)),
        bk_c=_col_chunks(np.asarray(L0_attn_bk, np.float32)),
        bv_c=_col_chunks(np.asarray(L0_attn_bv, np.float32)),
        bo_c=_col_chunks(np.asarray(L0_attn_bo, np.float32)),
        gs_c=_col_chunks(np.asarray(L0_ffn_gate_s, np.float32)),
        us_c=_col_chunks(np.asarray(L0_ffn_up_s, np.float32)),
        ds_c=_col_chunks(np.asarray(L0_ffn_down_s, np.float32)),
        cosf=cosf, sinf=sinf, maskt=mask, rpermt=rp, idbt=idb, idft=idf,
        epst=np.full((128, 1), EPS, dtype=np.float32),
    )

    if "nc" not in _BUILD_CACHE:
        _BUILD_CACHE["nc"] = build_nc()
    nc = _BUILD_CACHE["nc"]

    in_maps = []
    for core in range(NCORE):
        m = dict(shared)
        m["x_in"] = np.ascontiguousarray(x[core * SEQ_PER_CORE:(core + 1) * SEQ_PER_CORE])
        in_maps.append(m)

    res = run_bass_kernel_spmd(nc, in_maps, core_ids=list(range(NCORE)))
    out = np.empty((B, T, C), dtype=np.float32)
    for core in range(NCORE):
        out[core * SEQ_PER_CORE:(core + 1) * SEQ_PER_CORE] = res.results[core]["y_out"]
    return out
